# version 02 (final)
# baseline (speedup 1.0000x reference)
import numpy as np
import ml_dtypes
import concourse.bass as bass
import concourse.mybir as mybir
from concourse.bass_utils import run_bass_kernel_spmd
from contextlib import ExitStack

F32 = mybir.dt.float32
BF16 = mybir.dt.bfloat16
U8 = mybir.dt.uint8
AF = mybir.ActivationFunctionType
ALU = mybir.AluOpType

DEBUG = False
UPTO = 9
NCORES = 8
M = 1024
D = 4096
NI = 11008
DT = 32
NT = 86
TS = 512
EPS = 1e-5

GU_BLKS = [(i * 512, 512) for i in range(21)] + [(21 * 512, 256)]
DN_BLKS = [(i * 384, 384) for i in range(10)] + [(3840, 256)]
NQ = 2 * len(GU_BLKS)
NDB = len(DN_BLKS)

_inner = np.where(np.arange(128) < 64, 2 * np.arange(128), 2 * (np.arange(128) - 64) + 1)
P4 = np.concatenate([128 * b + _inner for b in range(D // 128)])
P11 = np.concatenate([128 * b + _inner for b in range(NI // 128)])


def _prep_qweight_bytes(qw, perm_cols, blks, ngrp):
    K8, N = qw.shape
    B = np.ascontiguousarray(qw[:, perm_cols]).view(np.uint8)
    B = B.reshape(K8, N, 4).transpose(0, 2, 1).reshape(K8 * 4, N)
    parts = []
    for (n0, w) in blks:
        blk = B[:, n0:n0 + w].reshape(ngrp, 64, w).transpose(1, 0, 2)
        parts.append(np.ascontiguousarray(blk).reshape(-1))
    return np.concatenate(parts)


def host_prep(inputs):
    gi_g = np.asarray(inputs["L0_ffn_gate_g_idx"])
    gi_u = np.asarray(inputs["L0_ffn_up_g_idx"])
    gi_d = np.asarray(inputs["L0_ffn_down_g_idx"])
    expect = np.arange(D) // 128
    if not (np.array_equal(gi_g, expect) and np.array_equal(gi_u, expect)
            and np.array_equal(gi_d, np.arange(NI) // 128)):
        raise ValueError("kernel assumes contiguous 128-wide GPTQ groups (g_idx = arange//128)")
    x = np.ascontiguousarray(inputs["L0_attn_x"], dtype=np.float32).reshape(8192, D)
    xT = np.ascontiguousarray(x.T[P4, :])
    w1p = inputs["L0_attn_w1"][:, P4]
    w1r = np.ascontiguousarray(
        w1p.reshape(DT, 128, DT, 128).transpose(2, 1, 0, 3)).reshape(DT * 128, D)
    w2T = np.ascontiguousarray(inputs["L0_attn_w2"].T)[:, P4]
    maskp = inputs["L0_attn_sparse_mask"][P4][:, P4]
    b2r = np.ascontiguousarray(inputs["L0_attn_b2"][P4].reshape(DT, 128).T)

    def selmat(n):
        a = np.zeros((n, n, 128), dtype=ml_dtypes.bfloat16)
        a[np.arange(n), np.arange(n), :] = 1
        return np.ascontiguousarray(a.reshape(n, n * 128))

    shared = dict(
        w1r=w1r, b2r=b2r,
        gbytes=_prep_qweight_bytes(inputs["L0_ffn_gate_qweight"], P11, GU_BLKS, 32),
        ubytes=_prep_qweight_bytes(inputs["L0_ffn_up_qweight"], P11, GU_BLKS, 32),
        dbytes=_prep_qweight_bytes(inputs["L0_ffn_down_qweight"], P4, DN_BLKS, NT),
        gsc=np.ascontiguousarray(inputs["L0_ffn_gate_scales"][:, P11]),
        usc=np.ascontiguousarray(inputs["L0_ffn_up_scales"][:, P11]),
        dsc=np.ascontiguousarray(inputs["L0_ffn_down_scales"][:, P4]),
        gz=np.ascontiguousarray(inputs["L0_ffn_gate_qzeros"]).view(np.uint8).reshape(32, 5504),
        uz=np.ascontiguousarray(inputs["L0_ffn_up_qzeros"]).view(np.uint8).reshape(32, 5504),
        dz=np.ascontiguousarray(inputs["L0_ffn_down_qzeros"]).view(np.uint8).reshape(NT, 2048),
        constv=np.broadcast_to(np.array([EPS, -0.5], dtype=np.float32), (128, 2)).copy(),
        ones_f=np.ones((1, 128), dtype=np.float32),
        onescol_bf=np.ones((128, 1), dtype=ml_dtypes.bfloat16),
        kron32T=np.ascontiguousarray(
            np.broadcast_to(np.eye(32, dtype=ml_dtypes.bfloat16)[None], (128, 32, 32))
        ).reshape(128, 32 * 32),
        kron86T=np.ascontiguousarray(
            np.broadcast_to(np.eye(NT, dtype=ml_dtypes.bfloat16)[None], (128, NT, NT))
        ).reshape(128, NT * NT),
        kron32sel=selmat(32),
        kron86sel=selmat(NT),
    )
    per_core = []
    for c in range(NCORES):
        pc = dict(
            x=np.ascontiguousarray(xT[:, M * c:M * (c + 1)]),
            w2t=np.ascontiguousarray(
                w2T[:, TS * c:TS * (c + 1)].reshape(DT, 128, TS).transpose(1, 0, 2)
            ).reshape(128, DT * TS),
            maskr=np.ascontiguousarray(maskp[:, TS * c:TS * (c + 1)]).reshape(DT * 128, TS),
        )
        pc.update(shared)
        per_core.append(pc)
    return per_core


class Sem:
    def __init__(self, stk, nc, name):
        self.h = stk.enter_context(nc.semaphore(name))
        self.v = 0

    def inc(self, n=1):
        self.v += n
        return self.v


def build_nc():
    nc = bass.Bass("TRN2", target_bir_lowering=False)
    gu_blen = sum(64 * 32 * w for (_, w) in GU_BLKS)
    dn_blen = sum(64 * NT * w for (_, w) in DN_BLKS)

    d_x = nc.dram_tensor("x", [D, M], F32, kind="ExternalInput").ap()
    d_w1r = nc.dram_tensor("w1r", [DT * 128, D], F32, kind="ExternalInput").ap()
    d_w2t = nc.dram_tensor("w2t", [128, DT * TS], F32, kind="ExternalInput").ap()
    d_mask = nc.dram_tensor("maskr", [DT * 128, TS], F32, kind="ExternalInput").ap()
    d_b2 = nc.dram_tensor("b2r", [128, DT], F32, kind="ExternalInput").ap()
    d_gb = nc.dram_tensor("gbytes", [gu_blen], U8, kind="ExternalInput").ap()
    d_ub = nc.dram_tensor("ubytes", [gu_blen], U8, kind="ExternalInput").ap()
    d_db = nc.dram_tensor("dbytes", [dn_blen], U8, kind="ExternalInput").ap()
    d_gsc = nc.dram_tensor("gsc", [32, NI], F32, kind="ExternalInput").ap()
    d_usc = nc.dram_tensor("usc", [32, NI], F32, kind="ExternalInput").ap()
    d_dsc = nc.dram_tensor("dsc", [NT, D], F32, kind="ExternalInput").ap()
    d_gz = nc.dram_tensor("gz", [32, 5504], U8, kind="ExternalInput").ap()
    d_uz = nc.dram_tensor("uz", [32, 5504], U8, kind="ExternalInput").ap()
    d_dz = nc.dram_tensor("dz", [NT, 2048], U8, kind="ExternalInput").ap()
    d_constv = nc.dram_tensor("constv", [128, 2], F32, kind="ExternalInput").ap()
    d_ones_f = nc.dram_tensor("ones_f", [1, 128], F32, kind="ExternalInput").ap()
    d_onescol = nc.dram_tensor("onescol_bf", [128, 1], BF16, kind="ExternalInput").ap()
    d_k32T = nc.dram_tensor("kron32T", [128, 32 * 32], BF16, kind="ExternalInput").ap()
    d_k86T = nc.dram_tensor("kron86T", [128, NT * NT], BF16, kind="ExternalInput").ap()
    d_k32sel = nc.dram_tensor("kron32sel", [32, 32 * 128], BF16, kind="ExternalInput").ap()
    d_k86sel = nc.dram_tensor("kron86sel", [NT, NT * 128], BF16, kind="ExternalInput").ap()
    d_out = nc.dram_tensor("out", [D, M], F32, kind="ExternalOutput").ap()

    sk = "ExternalOutput" if DEBUG else "Internal"
    d_z = nc.dram_tensor("z_dram", [D, M], F32, kind=sk).ap()
    d_o1 = nc.dram_tensor("out1_dram", [D, M], F32, kind=sk).ap()
    d_h = nc.dram_tensor("h_dram", [NT * 128, M], BF16, kind=sk).ap()
    d_ffn = nc.dram_tensor("ffn_dram", [D, M], BF16, kind=sk).ap()
    cc_in_h = nc.dram_tensor("cc_in", [D, TS], BF16, kind="Internal")
    cc_out_h = nc.dram_tensor("cc_out", [NCORES * D, TS], BF16, kind="Internal")

    dbg = {}
    if DEBUG:
        dbg["h0"] = nc.dram_tensor("dbg_h0", [128, DT * M], BF16, kind="ExternalOutput").ap()
        dbg["attn"] = nc.dram_tensor("dbg_attn", [128, DT * TS], BF16, kind="ExternalOutput").ap()
        dbg["xf"] = nc.dram_tensor("dbg_xf", [128, DT * M], BF16, kind="ExternalOutput").ap()
        dbg["fst0"] = nc.dram_tensor("dbg_fst0", [128, 512], BF16, kind="ExternalOutput").ap()

    stk = ExitStack()
    b2_sb = stk.enter_context(nc.sbuf_tensor("b2_sb", [128, DT], F32))
    ones_f = stk.enter_context(nc.sbuf_tensor("ones_f_sb", [1, 128], F32))
    onescol = stk.enter_context(nc.sbuf_tensor("onescol_sb", [128, 1], BF16))
    szpd = stk.enter_context(nc.sbuf_tensor("szpd", [NT, D], BF16))
    xgsT = stk.enter_context(nc.sbuf_tensor("xgsT", [64, M], BF16))
    hgs_sb = stk.enter_context(nc.sbuf_tensor("hgs_sb", [NT, M], BF16))
    constv_sb = stk.enter_context(nc.sbuf_tensor("constv_sb", [128, 2], F32))
    nc.const_aps.register(F32, EPS, constv_sb[:, 0:1])
    nc.const_aps.register(F32, -0.5, constv_sb[:, 1:2])
    PS = [stk.enter_context(nc.psum_tensor("ps%d" % i, [128, 1024], F32)) for i in range(4)]

    RW = [None, None, None]

    def alloc_rows(pstk, tag):
        RW[0] = pstk.enter_context(nc.sbuf_tensor("rowA_" + tag, [1, M], F32))
        RW[1] = pstk.enter_context(nc.sbuf_tensor("rowB_" + tag, [1, M], F32))
        RW[2] = pstk.enter_context(nc.sbuf_tensor("rowT_" + tag, [1, M], F32))

    S = {}
    for nm in ["TA", "TB", "TC", "VA", "VB", "VC", "SA", "SB", "CCS",
               "CN", "X1A", "X1B", "X2A", "X2B", "MKA", "MKB", "W1A", "W1B",
               "WT", "GPC", "DBG", "ATA", "ATB", "SPZ0", "SPZ1", "SPZ2",
               "ZXA", "ZXB", "SO0", "SO1", "O1A", "O1B", "ZL", "SCL",
               "BBA", "BBB", "SRA", "SRB", "SPH0", "SPH1", "SPH2", "SPH3",
               "BEA", "BEB", "HHA", "HHB", "SPF0", "SPF1", "FBA", "FBB",
               "FCA", "FCB", "GPO0", "GPO1"]:
        S[nm] = Sem(stk, nc, "sem_" + nm)
    TA, TB, TC = S["TA"], S["TB"], S["TC"]
    VA, VB, VC = S["VA"], S["VB"], S["VC"]
    SA, SB, CCS, CN = S["SA"], S["SB"], S["CCS"], S["CN"]
    X1 = [S["X1A"], S["X1B"]]
    X2 = [S["X2A"], S["X2B"]]
    MK2 = [S["MKA"], S["MKB"]]
    W1S = [S["W1A"], S["W1B"]]
    WT, GPC, DBGS = S["WT"], S["GPC"], S["DBG"]
    AT = [S["ATA"], S["ATB"]]
    SPZ = [S["SPZ0"], S["SPZ1"], S["SPZ2"]]
    ZX = [S["ZXA"], S["ZXB"]]
    SO = [S["SO0"], S["SO1"]]
    O1 = [S["O1A"], S["O1B"]]
    ZL, SCL = S["ZL"], S["SCL"]
    BB = [S["BBA"], S["BBB"]]
    SR = [S["SRA"], S["SRB"]]
    SPH = [S["SPH0"], S["SPH1"], S["SPH2"], S["SPH3"]]
    BE = [S["BEA"], S["BEB"]]
    HH = [S["HHA"], S["HHB"]]
    SPF = [S["SPF0"], S["SPF1"]]
    FB = [S["FBA"], S["FBB"]]
    FC = [S["FCA"], S["FCB"]]
    GPO = [S["GPO0"], S["GPO1"]]

    gu_boff, off = [], 0
    for (_, w) in GU_BLKS:
        gu_boff.append(off); off += 64 * 32 * w
    dn_boff, off = [], 0
    for (_, w) in DN_BLKS:
        dn_boff.append(off); off += 64 * NT * w

    MKW = dict(skip_group_check=True)

    def rows_v(vector, n_red):
        rowA, rowB, rowT = RW
        vector.tensor_scalar(rowB[:, :], PS[0][0:1, 0:M], -1.0 / n_red, None, ALU.mult)
        vector.tensor_scalar(rowT[:, :], PS[0][32:33, 0:M], 1.0 / n_red, None, ALU.mult)
        vector.tensor_tensor(rowA[:, :], rowB[:, :], rowB[:, :], ALU.mult)
        vector.tensor_sub(rowT[:, :], rowT[:, :], rowA[:, :]).then_inc(VC.h, 1)
        return VC.inc()

    def rows_s(scalar):
        rowA, rowB, rowT = RW
        scalar.activation(rowT[:, :], rowT[:, :], AF.Ln, bias=EPS)
        scalar.activation(rowA[:, :], rowT[:, :], AF.Exp, scale=-0.5).then_inc(SB.h, 1)
        return SB.inc()

    def rows_t(tensor):
        rowA, rowB, rowT = RW
        for mc in range(2):
            tensor.matmul(PS[2][:, 512 * mc:512 * (mc + 1)], ones_f[0:1, :],
                          rowA[0:1, 512 * mc:512 * (mc + 1)], start=True, stop=True, **MKW)
        for mc in range(2):
            mm = tensor.matmul(PS[3][:, 512 * mc:512 * (mc + 1)], ones_f[0:1, :],
                               rowB[0:1, 512 * mc:512 * (mc + 1)], start=True, stop=True, **MKW)
        mm.then_inc(TB.h, 1)
        return TB.inc()

    def bc_evac(vector, a, b):
        vector.tensor_copy(a[:, :], PS[2][:, 0:M])
        vector.tensor_copy(b[:, :], PS[3][:, 0:M]).then_inc(VC.h, 1)
        return VC.inc()

    def stat_mms(tensor, rhs_val, rhs_sq, first, last):
        for mc in range(2):
            tensor.matmul(PS[0][0:1, 512 * mc:512 * (mc + 1)], onescol[:, 0:1],
                          rhs_val[:, 512 * mc:512 * (mc + 1)], start=first, stop=last, **MKW)
        for mc in range(2):
            mm = tensor.matmul(PS[0][32:33, 512 * mc:512 * (mc + 1)], onescol[:, 0:1],
                               rhs_sq[:, 512 * mc:512 * (mc + 1)], start=first, stop=last, **MKW)
        return mm

    stkRA = ExitStack()
    regA = stkRA.enter_context(nc.sbuf_tensor("regA", [128, DT * M], BF16))

    with ExitStack() as pa:
        xb = [pa.enter_context(nc.sbuf_tensor("xb%d" % i, [128, M], F32)) for i in range(2)]
        xbf = [pa.enter_context(nc.sbuf_tensor("xbf%d" % i, [128, M], BF16)) for i in range(2)]
        xsq = [pa.enter_context(nc.sbuf_tensor("xsq%d" % i, [128, M], BF16)) for i in range(2)]
        w1p = [pa.enter_context(nc.sbuf_tensor("w1p%d" % i, [128, D], BF16)) for i in range(2)]
        w2tb = pa.enter_context(nc.sbuf_tensor("w2tb", [128, DT * TS], BF16))
        attn_own = pa.enter_context(nc.sbuf_tensor("attn_own", [128, DT * TS], BF16))
        mkb = [pa.enter_context(nc.sbuf_tensor("mkb%d" % i, [128, TS], F32)) for i in range(2)]
        bcA = pa.enter_context(nc.sbuf_tensor("bcA_a", [128, M], F32))
        bcB = pa.enter_context(nc.sbuf_tensor("bcB_a", [128, M], F32))
        alloc_rows(pa, "a")

        va_cast = [VA.v + t + 1 for t in range(DT)]
        va_h0 = [va_cast[-1] + t + 1 for t in range(DT)]
        vb_ev = [VB.v + t + 1 for t in range(DT)]
        vc_rows1 = VC.v + 1
        sa_sq = [SA.v + t + 1 for t in range(DT)]
        sb_rstd1 = SB.v + 1
        ta_st = [TA.v + t + 1 for t in range(DT)]
        tb_bc1 = TB.v + 1
        tc_at = [TC.v + t + 1 for t in range(DT)]
        cn_done = CN.v + 16 * 4
        x1v = [X1[t % 2].v + 16 * (t // 2 + 1) for t in range(DT)]
        x2v = [X2[t % 2].v + 16 * (t // 2 + 1) for t in range(DT)]
        mkv = [MK2[t % 2].v + 16 * (t // 2 + 1) for t in range(DT)]
        w1v = [W1S[t % 2].v + 16 * (t // 2 + 1) for t in range(DT)]
        wt_done = WT.v + 16 * 4
        gpc_cci = GPC.v + 16
        ccs_done = CCS.v + 1

        with nc.Block() as blk:
            @blk.sync
            def _(sync):
                for ap_d, ap_s in [(b2_sb[:, :], d_b2), (ones_f[:, :], d_ones_f),
                                   (onescol[:, :], d_onescol), (constv_sb[:, :], d_constv)]:
                    sync.dma_start(ap_d, ap_s).then_inc(CN.h, 16)
                CN.v = cn_done
                for t in range(DT):
                    if t >= 2:
                        sync.wait_ge(VA.h, va_cast[t - 2])
                    sync.dma_start(xb[t % 2][:, :], d_x[128 * t:128 * (t + 1), :]).then_inc(X1[t % 2].h, 16)
                X1[0].v, X1[1].v = x1v[-2], x1v[-1]
                for t in range(DT):
                    if t >= 2:
                        sync.wait_ge(VB.h, vb_ev[t - 2])
                    sync.dma_start(mkb[t % 2][:, :], d_mask[128 * t:128 * (t + 1), :]).then_inc(MK2[t % 2].h, 16)
                MK2[0].v, MK2[1].v = mkv[-2], mkv[-1]
                for t in range(DT):
                    sync.wait_ge(VA.h, va_h0[t - 2] if t >= 2 else va_cast[t])
                    sync.dma_start(xb[t % 2][:, :], d_x[128 * t:128 * (t + 1), :]).then_inc(X2[t % 2].h, 16)
                X2[0].v, X2[1].v = x2v[-2], x2v[-1]

            @blk.gpsimd
            def _(g):
                for ch in range(4):
                    g.dma_start(w2tb[:, 4096 * ch:4096 * (ch + 1)],
                                d_w2t[:, 4096 * ch:4096 * (ch + 1)]).then_inc(WT.h, 16)
                WT.v = wt_done
                for t in range(DT):
                    if t >= 2:
                        g.wait_ge(TC.h, tc_at[t - 2])
                    g.dma_start(w1p[t % 2][:, :], d_w1r[128 * t:128 * (t + 1), :]).then_inc(W1S[t % 2].h, 16)
                W1S[0].v, W1S[1].v = w1v[-2], w1v[-1]
                g.wait_ge(VB.h, vb_ev[-1])
                g.dma_start(cc_in_h.ap().rearrange("(st p) c -> p st c", p=128),
                            attn_own[:, :].rearrange("p (st c) -> p st c", c=TS)).then_inc(GPC.h, 16)
                GPC.v = gpc_cci
                g.wait_ge(GPC.h, gpc_cci)
                g.collective_compute(
                    "AllGather", ALU.bypass, replica_groups=[list(range(NCORES))],
                    ins=[cc_in_h.ap().opt()], outs=[cc_out_h.ap().opt()],
                ).then_inc(CCS.h, 1)
                CCS.v = ccs_done
                if DEBUG:
                    g.wait_ge(VA.h, va_h0[-1])
                    g.dma_start(dbg["h0"], regA[:, :]).then_inc(DBGS.h, 16)
                    g.dma_start(dbg["attn"], attn_own[:, :]).then_inc(DBGS.h, 16)
                    DBGS.v += 32
                    g.wait_ge(DBGS.h, DBGS.v)

            @blk.vector
            def _(vector):
                for t in range(DT):
                    vector.wait_ge(X1[t % 2].h, x1v[t])
                    if t >= 2:
                        vector.wait_ge(TA.h, ta_st[t - 2])
                    vector.tensor_copy(xbf[t % 2][:, :], xb[t % 2][:, :]).then_inc(VA.h, 1)
                VA.v = va_cast[-1]
                for t in range(DT):
                    vector.wait_ge(TC.h, tc_at[t])
                    vector.wait_ge(MK2[t % 2].h, mkv[t])
                    vector.tensor_tensor(
                        attn_own[:, TS * t:TS * (t + 1)],
                        PS[1][:, 512 * (t % 2):512 * (t % 2) + 512],
                        mkb[t % 2][:, :], ALU.mult).then_inc(VB.h, 1)
                VB.v = vb_ev[-1]
                vector.wait_ge(TA.h, ta_st[-1])
                rows_v(vector, D)
                vector.wait_ge(TB.h, tb_bc1)
                bc_evac(vector, bcA, bcB)
                for t in range(DT):
                    vector.wait_ge(X2[t % 2].h, x2v[t])
                    vector.tensor_tensor(xb[t % 2][:, :], xb[t % 2][:, :], bcB[:, :], ALU.add)
                    vector.tensor_tensor(regA[:, M * t:M * (t + 1)], xb[t % 2][:, :],
                                         bcA[:, :], ALU.mult).then_inc(VA.h, 1)
                VA.v = va_h0[-1]

            @blk.scalar
            def _(scalar):
                for t in range(DT):
                    scalar.wait_ge(VA.h, va_cast[t])
                    if t >= 2:
                        scalar.wait_ge(TA.h, ta_st[t - 2])
                    scalar.activation(xsq[t % 2][:, :], xbf[t % 2][:, :], AF.Square).then_inc(SA.h, 1)
                SA.v = sa_sq[-1]
                scalar.wait_ge(VC.h, vc_rows1)
                scalar.wait_ge(CN.h, cn_done)
                rows_s(scalar)

            @blk.tensor
            def _(tensor):
                tensor.wait_ge(CN.h, cn_done)
                for t in range(DT):
                    tensor.wait_ge(SA.h, sa_sq[t])
                    stat_mms(tensor, xbf[t % 2], xsq[t % 2], t == 0, t == DT - 1).then_inc(TA.h, 1)
                TA.v = ta_st[-1]
                tensor.wait_ge(WT.h, wt_done)
                for t in range(DT):
                    tensor.wait_ge(W1S[t % 2].h, w1v[t])
                    if t >= 2:
                        tensor.wait_ge(VB.h, vb_ev[t - 2])
                    half = 512 * (t % 2)
                    for dk in range(DT):
                        mm = tensor.matmul(PS[1][:, half:half + 512],
                                           w1p[t % 2][:, 128 * dk:128 * (dk + 1)],
                                           w2tb[:, 512 * dk:512 * (dk + 1)],
                                           start=(dk == 0), stop=(dk == DT - 1), **MKW)
                    mm.then_inc(TC.h, 1)
                TC.v = tc_at[-1]
                tensor.wait_ge(SB.h, sb_rstd1)
                tensor.wait_ge(VC.h, vc_rows1)
                rows_t(tensor)

    if UPTO <= 1:
        return nc

    with ExitStack() as pb:
        atb = [pb.enter_context(nc.sbuf_tensor("atb%d" % i, [128, DT * TS], BF16)) for i in range(2)]
        z3 = [pb.enter_context(nc.sbuf_tensor("z3_%d" % i, [128, M], F32)) for i in range(3)]
        zbf = [pb.enter_context(nc.sbuf_tensor("zbf%d" % i, [128, M], BF16)) for i in range(2)]
        zsq = [pb.enter_context(nc.sbuf_tensor("zsq%d" % i, [128, M], BF16)) for i in range(2)]

        NK = 32
        tb_y = [TB.v + k + 1 for k in range(NK)]
        tc_st = [TC.v + k + 1 for k in range(NK)]
        vb_cast = [VB.v + k + 1 for k in range(NK)]
        sa_gelu, sa_sqz = {}, {}
        ev = SA.v
        for k in range(NK):
            ev += 1
            sa_gelu[k] = ev
            if k >= 1:
                ev += 1
                sa_sqz[k - 1] = ev
        ev += 1
        sa_sqz[NK - 1] = ev
        sa_endB = ev
        atv = [AT[e % 2].v + 16 * (e // 2 + 1) for e in range(8)]
        spzv = [SPZ[k % 3].v + 16 * (k // 3 + 1) for k in range(NK)]

        with nc.Block() as blk:
            @blk.sync
            def _(sync):
                sync.wait_ge(CCS.h, ccs_done)
                cco = cc_out_h.ap().rearrange("(e st p) c -> e p st c", e=8, p=128)
                for e in range(8):
                    if e >= 2:
                        sync.wait_ge(TB.h, tb_y[4 * (e - 1) - 1])
                    sync.dma_start(atb[e % 2][:, :].rearrange("p (st c) -> p st c", c=TS),
                                   cco[e]).then_inc(AT[e % 2].h, 16)
                AT[0].v, AT[1].v = atv[-2], atv[-1]

            @blk.scalar
            def _(scalar):
                def sq_spill(km):
                    scalar.wait_ge(VB.h, vb_cast[km])
                    if km >= 2:
                        scalar.wait_ge(TC.h, tc_st[km - 2])
                    scalar.activation(zsq[km % 2][:, :], zbf[km % 2][:, :], AF.Square).then_inc(SA.h, 1)
                    scalar.dma_start(d_z[128 * km:128 * (km + 1), :],
                                     z3[km % 3][:, :]).then_inc(SPZ[km % 3].h, 16)
                for k in range(NK):
                    scalar.wait_ge(TB.h, tb_y[k])
                    if k >= 3:
                        scalar.wait_ge(VB.h, vb_cast[k - 3])
                        scalar.wait_ge(SPZ[k % 3].h, spzv[k - 3])
                    psY = PS[1] if k % 2 == 0 else PS[2]
                    scalar.activation(z3[k % 3][:, :], psY[:, 0:M], AF.Gelu,
                                      bias=b2_sb[:, k:k + 1]).then_inc(SA.h, 1)
                    if k >= 1:
                        sq_spill(k - 1)
                sq_spill(NK - 1)
                SA.v = sa_endB
                for i in range(3):
                    SPZ[i].v = max(spzv[j] for j in range(NK) if j % 3 == i)

            @blk.vector
            def _(vector):
                for k in range(NK):
                    vector.wait_ge(SA.h, sa_gelu[k])
                    if k >= 2:
                        vector.wait_ge(TC.h, tc_st[k - 2])
                    vector.tensor_copy(zbf[k % 2][:, :], z3[k % 3][:, :]).then_inc(VB.h, 1)
                VB.v = vb_cast[-1]

            @blk.tensor
            def _(tensor):
                def stats_for(km):
                    tensor.wait_ge(SA.h, sa_sqz[km])
                    stat_mms(tensor, zbf[km % 2], zsq[km % 2], km == 0, km == NK - 1).then_inc(TC.h, 1)
                for k in range(NK):
                    e, tt = k // 4, k % 4
                    tensor.wait_ge(AT[e % 2].h, atv[e])
                    if k >= 2:
                        tensor.wait_ge(SA.h, sa_gelu[k - 2])
                    psY = PS[1] if k % 2 == 0 else PS[2]
                    for st in range(DT):
                        lhs = atb[e % 2][:, TS * st + 128 * tt:TS * st + 128 * (tt + 1)]
                        for mc in range(2):
                            mm = tensor.matmul(psY[:, 512 * mc:512 * (mc + 1)], lhs,
                                               regA[:, M * st + 512 * mc:M * st + 512 * (mc + 1)],
                                               start=(st == 0), stop=(st == DT - 1), **MKW)
                    mm.then_inc(TB.h, 1)
                    if k >= 2:
                        stats_for(k - 2)
                stats_for(NK - 2)
                stats_for(NK - 1)
                TB.v = tb_y[-1]
                TC.v = tc_st[-1]

    if UPTO <= 2:
        return nc

    stkR2 = ExitStack()
    szp_gu = stkR2.enter_context(nc.sbuf_tensor("szp_gu", [64, NI], BF16))
    k32T = stkR2.enter_context(nc.sbuf_tensor("k32T", [128, 32 * 32], BF16))

    with ExitStack() as pc:
        zb = [pc.enter_context(nc.sbuf_tensor("zb%d" % i, [128, M], F32)) for i in range(2)]
        xb2 = [pc.enter_context(nc.sbuf_tensor("xb2_%d" % i, [128, M], F32)) for i in range(2)]
        o1b = [pc.enter_context(nc.sbuf_tensor("o1b%d" % i, [128, M], F32)) for i in range(2)]
        obf = [pc.enter_context(nc.sbuf_tensor("obf%d" % i, [128, M], BF16)) for i in range(2)]
        osq = [pc.enter_context(nc.sbuf_tensor("osq%d" % i, [128, M], BF16)) for i in range(2)]
        bcA2 = pc.enter_context(nc.sbuf_tensor("bcA_c", [128, M], F32))
        bcB2 = pc.enter_context(nc.sbuf_tensor("bcB_c", [128, M], F32))
        zst = pc.enter_context(nc.sbuf_tensor("zst", [NT, 5504], U8))
        z4t = pc.enter_context(nc.sbuf_tensor("z4t", [NT, NI], U8))
        zf32 = pc.enter_context(nc.sbuf_tensor("zf32", [NT, 2752], F32))
        scf = pc.enter_context(nc.sbuf_tensor("scf", [NT, 2752], F32))
        alloc_rows(pc, "c")

        SZP_JOBS = []
        for (szp_ap, nsc, nz, gc, rb) in [
                (lambda a, b: szp_gu[0:32, a:b], NI, 5504, 32, 0),
                (lambda a, b: szp_gu[32:64, a:b], NI, 5504, 32, 32),
                (lambda a, b: szpd[0:NT, a:b], D, 2048, NT, 0)]:
            cw = 2752 if nsc == NI else 2048
            chunks = [(c0, min(cw, nsc - c0)) for c0 in range(0, nsc, cw)]
            SZP_JOBS.append((szp_ap, nsc, nz, gc, rb, chunks))

        vc_rows2 = VC.v + 1
        vc_bce2 = VC.v + 2
        sb_rstd2 = SB.v + 1
        tb_bc2 = TB.v + 1
        va_o1 = [VA.v + t + 1 for t in range(DT)]
        sa_sq2 = [SA.v + t + 1 for t in range(DT)]
        ta_st2 = [TA.v + t + 1 for t in range(DT)]
        sov = [SO[t % 2].v + 16 * (t // 2 + 1) for t in range(DT)]
        so_tot = [SO[0].v + 16 * 16, SO[1].v + 16 * 16]
        vc_rows3 = vc_bce2 + 1
        vc_bce3 = vc_bce2 + 2
        sb_rstd3 = sb_rstd2 + 1
        tb_bc3 = tb_bc2 + 1
        vb_xf = [VB.v + t + 1 for t in range(DT)]
        tc_xgs = [TC.v + t + 1 for t in range(DT)]
        vc_xgse = vc_bce3 + 1
        vc_szp = {}
        ev = vc_xgse
        for ji, job in enumerate(SZP_JOBS):
            for ci in range(len(job[5])):
                ev += 1
                vc_szp[(ji, ci)] = ev
        zxv = [ZX[t % 2].v + 32 * (t // 2 + 1) for t in range(DT)]
        o1v = [O1[t % 2].v + 16 * (t // 2 + 1) for t in range(DT)]
        zlv = [ZL.v + 16 * (ji + 1) for ji in range(3)]
        sclv = {}
        ev = SCL.v
        for ji, job in enumerate(SZP_JOBS):
            for ci in range(len(job[5])):
                ev += 16
                sclv[(ji, ci)] = ev
        spz_tot = [SPZ[i].v for i in range(3)]
        kc32_done = CN.v + 16

        with nc.Block() as blk:
            @blk.sync
            def _(sync):
                sync.dma_start(k32T[:, :], d_k32T).then_inc(CN.h, 16)
                CN.v = kc32_done
                for i in range(3):
                    sync.wait_ge(SPZ[i].h, spz_tot[i])
                for t in range(DT):
                    if t >= 2:
                        sync.wait_ge(VA.h, va_o1[t - 2])
                    sync.dma_start(zb[t % 2][:, :], d_z[128 * t:128 * (t + 1), :]).then_inc(ZX[t % 2].h, 16)
                    sync.dma_start(xb2[t % 2][:, :], d_x[128 * t:128 * (t + 1), :]).then_inc(ZX[t % 2].h, 16)
                ZX[0].v, ZX[1].v = zxv[-2], zxv[-1]
                sync.wait_ge(SO[0].h, so_tot[0])
                sync.wait_ge(SO[1].h, so_tot[1])
                for t in range(DT):
                    if t >= 2:
                        sync.wait_ge(VB.h, vb_xf[t - 2])
                    sync.dma_start(o1b[t % 2][:, :], d_o1[128 * t:128 * (t + 1), :]).then_inc(O1[t % 2].h, 16)
                O1[0].v, O1[1].v = o1v[-2], o1v[-1]

            @blk.gpsimd
            def _(g):
                for ji, (szp_ap, nsc, nz, gc, rb, chunks) in enumerate(SZP_JOBS):
                    z_d = [d_gz, d_uz, d_dz][ji]
                    sc_d = [d_gsc, d_usc, d_dsc][ji]
                    g.dma_start(zst[rb:rb + gc, 0:nz], z_d).then_inc(ZL.h, 16)
                    for ci, (c0, cw) in enumerate(chunks):
                        if (ji, ci) != (0, 0):
                            pj, pcn = (ji, ci - 1) if ci >= 1 else (ji - 1, len(SZP_JOBS[ji - 1][5]) - 1)
                            g.wait_ge(VC.h, vc_szp[(pj, pcn)])
                        g.dma_start(scf[rb:rb + gc, 0:cw], sc_d[0:gc, c0:c0 + cw]).then_inc(SCL.h, 16)
                ZL.v = zlv[-1]
                SCL.v = sclv[(2, len(SZP_JOBS[2][5]) - 1)]
                if DEBUG:
                    g.wait_ge(VB.h, vb_xf[-1])
                    g.dma_start(dbg["xf"], regA[:, :]).then_inc(DBGS.h, 16)
                    DBGS.v += 16
                    g.wait_ge(DBGS.h, DBGS.v)

            @blk.vector
            def _(vector):
                rows_v(vector, D)
                vector.wait_ge(TB.h, tb_bc2)
                bc_evac(vector, bcA2, bcB2)
                for t in range(DT):
                    vector.wait_ge(ZX[t % 2].h, zxv[t])
                    if t >= 2:
                        vector.wait_ge(TA.h, ta_st2[t - 2])
                        vector.wait_ge(SO[t % 2].h, sov[t - 2])
                    vector.tensor_tensor(zb[t % 2][:, :], zb[t % 2][:, :], bcB2[:, :], ALU.add)
                    vector.tensor_tensor(zb[t % 2][:, :], zb[t % 2][:, :], bcA2[:, :], ALU.mult)
                    vector.tensor_tensor(o1b[t % 2][:, :], zb[t % 2][:, :], xb2[t % 2][:, :], ALU.add)
                    vector.tensor_copy(obf[t % 2][:, :], o1b[t % 2][:, :]).then_inc(VA.h, 1)
                VA.v = va_o1[-1]
                vector.wait_ge(TA.h, ta_st2[-1])
                rows_v(vector, D)
                vector.wait_ge(TB.h, tb_bc3)
                bc_evac(vector, bcA2, bcB2)
                for t in range(DT):
                    vector.wait_ge(O1[t % 2].h, o1v[t])
                    vector.tensor_tensor(o1b[t % 2][:, :], o1b[t % 2][:, :], bcB2[:, :], ALU.add)
                    vector.tensor_tensor(regA[:, M * t:M * (t + 1)], o1b[t % 2][:, :],
                                         bcA2[:, :], ALU.mult).then_inc(VB.h, 1)
                VB.v = vb_xf[-1]
                vector.wait_ge(TC.h, tc_xgs[-1])
                vector.tensor_copy(xgsT[0:64, :], PS[1][0:64, 0:M]).then_inc(VC.h, 1)
                VC.inc()
                for ji, (szp_ap, nsc, nz, gc, rb, chunks) in enumerate(SZP_JOBS):
                    vector.wait_ge(ZL.h, zlv[ji])
                    src = zst[rb:rb + gc, 0:nz].rearrange("g (b c) -> g b c", c=64)
                    dst = z4t[rb:rb + gc, 0:nsc].rearrange("g (b c) -> g b c", c=128)
                    vector.tensor_scalar(dst[:, :, 0:64], src, 15, None, ALU.bitwise_and)
                    vector.tensor_scalar(dst[:, :, 64:128], src, 4, None, ALU.logical_shift_right)
                    for ci, (c0, cw) in enumerate(chunks):
                        vector.wait_ge(SCL.h, sclv[(ji, ci)])
                        vector.tensor_scalar(zf32[rb:rb + gc, 0:cw], z4t[rb:rb + gc, c0:c0 + cw],
                                             1.0, -1.0, ALU.add, ALU.mult)
                        vector.tensor_tensor(szp_ap(c0, c0 + cw), zf32[rb:rb + gc, 0:cw],
                                             scf[rb:rb + gc, 0:cw], ALU.mult).then_inc(VC.h, 1)
                        VC.inc()

            @blk.scalar
            def _(scalar):
                scalar.wait_ge(VC.h, vc_rows2)
                rows_s(scalar)
                for t in range(DT):
                    scalar.wait_ge(VA.h, va_o1[t])
                    if t >= 2:
                        scalar.wait_ge(TA.h, ta_st2[t - 2])
                    scalar.activation(osq[t % 2][:, :], obf[t % 2][:, :], AF.Square).then_inc(SA.h, 1)
                    scalar.dma_start(d_o1[128 * t:128 * (t + 1), :], o1b[t % 2][:, :]).then_inc(SO[t % 2].h, 16)
                SA.v = sa_sq2[-1]
                SO[0].v, SO[1].v = so_tot[0], so_tot[1]
                scalar.wait_ge(VC.h, vc_rows3)
                rows_s(scalar)

            @blk.tensor
            def _(tensor):
                tensor.wait_ge(SB.h, sb_rstd2)
                tensor.wait_ge(VC.h, vc_rows2)
                rows_t(tensor)
                for t in range(DT):
                    tensor.wait_ge(SA.h, sa_sq2[t])
                    stat_mms(tensor, obf[t % 2], osq[t % 2], t == 0, t == DT - 1).then_inc(TA.h, 1)
                TA.v = ta_st2[-1]
                tensor.wait_ge(SB.h, sb_rstd3)
                tensor.wait_ge(VC.h, vc_rows3)
                rows_t(tensor)
                tensor.wait_ge(CN.h, kc32_done)
                for t in range(DT):
                    tensor.wait_ge(VB.h, vb_xf[t])
                    for base in (0, 32):
                        for mc in range(2):
                            mm = tensor.matmul(PS[1][base:base + 32, 512 * mc:512 * (mc + 1)],
                                               k32T[:, 32 * t:32 * (t + 1)],
                                               regA[:, M * t + 512 * mc:M * t + 512 * (mc + 1)],
                                               start=(t == 0), stop=(t == DT - 1), **MKW)
                    mm.then_inc(TC.h, 1)
                TC.v = tc_xgs[-1]

    if UPTO <= 3:
        return nc

    with ExitStack() as pd:
        k86T = pd.enter_context(nc.sbuf_tensor("k86T", [128, NT * NT], BF16))
        k32sel = pd.enter_context(nc.sbuf_tensor("k32sel", [32, 32 * 128], BF16))
        wblk = [pd.enter_context(nc.sbuf_tensor("wblk%d" % i, [128, 32 * 512], BF16)) for i in range(2)]
        bbD = [pd.enter_context(nc.sbuf_tensor("bbD%d" % i, [128, 4 * 512], U8)) for i in range(2)]
        srowK = [pd.enter_context(nc.sbuf_tensor("srowK%d" % i, [32, 512], BF16)) for i in range(2)]
        gsb = pd.enter_context(nc.sbuf_tensor("gsb", [128, 4 * M], BF16))

        jt_of = [GU_BLKS[q // 2][1] // 128 for q in range(NQ)]
        slots = [(q, j) for q in range(NQ) for j in range(jt_of[q])]
        sidx = {s: i for i, s in enumerate(slots)}
        upslots = [s for s in slots if s[0] % 2 == 1]
        uidx = {s: i for i, s in enumerate(upslots)}
        BCH = [(g0, 4) for g0 in range(0, 32, 4)]

        def jj_of(q, j):
            return GU_BLKS[q // 2][0] // 128 + j

        tb_sbc = [TB.v + i + 1 for i in range(NQ * 32)]
        va_mul = [VA.v + i + 1 for i in range(NQ * 32)]
        ta_mm = {}
        ev = TA.v
        for s in slots:
            ev += 1
            ta_mm[s] = ev
        sa_silu, vc_h = {}, {}
        eva, evc = SA.v, VC.v
        for s in slots:
            if s[0] % 2 == 0:
                eva += 1
                sa_silu[s] = eva
            else:
                evc += 1
                vc_h[s] = evc
        tc_hgs = {}
        ev = TC.v
        for s in upslots:
            ev += 1
            tc_hgs[s] = ev
        bbv = {}
        _cnt = [0, 0]
        for q in range(NQ):
            for bc in range(len(BCH)):
                p = bc % 2
                _cnt[p] += 1
                bbv[(q, bc)] = BB[p].v + 32 * _cnt[p]
        srv = [SR[q % 2].v + 16 * (q // 2 + 1) for q in range(NQ)]
        sphv = {}
        _cj = [0, 0, 0, 0]
        for s in upslots:
            _cj[s[1]] += 1
            sphv[s] = SPH[s[1]].v + 16 * _cj[s[1]]
        kc_done = CN.v + 32

        def psq(i):
            return PS[1] if i % 2 == 0 else PS[2]

        with nc.Block() as blk:
            @blk.sync
            def _(sync):
                sync.dma_start(k86T[:, :], d_k86T).then_inc(CN.h, 16)
                sync.dma_start(k32sel[:, :], d_k32sel).then_inc(CN.h, 16)
                CN.v = kc_done
                for q in range(NQ):
                    nb, mat = q // 2, q % 2
                    n0, w = GU_BLKS[nb]
                    dby = d_ub if mat else d_gb
                    src_all = dby[gu_boff[nb]:gu_boff[nb] + 64 * 32 * w].rearrange("(p f) -> p f", p=64)
                    for bc, (g0, cnt) in enumerate(BCH):
                        gi = q * len(BCH) + bc
                        if gi >= 2:
                            pq, pbc = divmod(gi - 2, len(BCH))
                            lastg = BCH[pbc][0] + BCH[pbc][1] - 1
                            sync.wait_ge(VA.h, va_mul[32 * pq + lastg])
                        src = src_all[:, g0 * w:(g0 + cnt) * w]
                        sync.dma_start(bbD[bc % 2][0:64, 0:cnt * w], src).then_inc(BB[bc % 2].h, 16)
                        sync.dma_start(bbD[bc % 2][64:128, 0:cnt * w], src).then_inc(BB[bc % 2].h, 16)
                BB[0].v = max(v for k2, v in bbv.items() if k2[1] % 2 == 0)
                BB[1].v = max(v for k2, v in bbv.items() if k2[1] % 2 == 1)

            @blk.gpsimd
            def _(g):
                for q in range(NQ):
                    nb, mat = q // 2, q % 2
                    n0, w = GU_BLKS[nb]
                    dsc_src = d_usc if mat else d_gsc
                    if q >= 2:
                        g.wait_ge(TB.h, tb_sbc[32 * (q - 1) - 1])
                    g.dma_start(srowK[q % 2][0:32, 0:w],
                                dsc_src[0:32, n0:n0 + w]).then_inc(SR[q % 2].h, 16)
                SR[0].v, SR[1].v = srv[-2], srv[-1]

            @blk.vector
            def _(vector):
                def hmult(qp, j):
                    s = (qp, j)
                    vector.wait_ge(TA.h, ta_mm[s])
                    vector.wait_ge(SA.h, sa_silu[(qp - 1, j)])
                    vector.tensor_tensor(gsb[:, M * j:M * (j + 1)], gsb[:, M * j:M * (j + 1)],
                                         psq(sidx[s])[:, 0:M], ALU.mult).then_inc(VC.h, 1)
                for q in range(NQ):
                    nb, mat = q // 2, q % 2
                    n0, w = GU_BLKS[nb]
                    qp = q - 1
                    hjobs = {}
                    if q >= 1 and qp % 2 == 1:
                        step = len(BCH) // jt_of[qp]
                        hjobs = {step * (j + 1) - 1: j for j in range(jt_of[qp])}
                    if q >= 2:
                        vector.wait_ge(TA.h, ta_mm[(q - 2, jt_of[q - 2] - 1)])
                    for bc, (g0, cnt) in enumerate(BCH):
                        vector.wait_ge(BB[bc % 2].h, bbv[(q, bc)])
                        vector.tensor_scalar(bbD[bc % 2][0:64, 0:cnt * w], bbD[bc % 2][0:64, 0:cnt * w],
                                             15, None, ALU.bitwise_and)
                        vector.tensor_scalar(bbD[bc % 2][64:128, 0:cnt * w], bbD[bc % 2][64:128, 0:cnt * w],
                                             4, None, ALU.logical_shift_right)
                        for gi2 in range(cnt):
                            gk = g0 + gi2
                            ib = 32 * q + gk
                            vector.wait_ge(TB.h, tb_sbc[ib])
                            half = 512 * (gk % 2)
                            vector.tensor_tensor(wblk[q % 2][:, gk * w:(gk + 1) * w],
                                                 bbD[bc % 2][:, gi2 * w:(gi2 + 1) * w],
                                                 PS[3][:, half:half + w], ALU.mult).then_inc(VA.h, 1)
                        if bc in hjobs:
                            hmult(qp, hjobs[bc])
                for j in range(jt_of[NQ - 1]):
                    hmult(NQ - 1, j)
                VA.v = va_mul[-1]
                VC.v = evc
                vector.wait_ge(TC.h, tc_hgs[upslots[-1]])
                vector.tensor_copy(hgs_sb[0:NT, :], PS[0][0:NT, 0:M]).then_inc(VC.h, 1)
                VC.inc()

            @blk.scalar
            def _(scalar):
                for s in slots:
                    q, j = s
                    if q % 2 == 0:
                        scalar.wait_ge(TA.h, ta_mm[s])
                        if q >= 1:
                            scalar.wait_ge(TC.h, tc_hgs[(q - 1, j)])
                            scalar.wait_ge(SPH[j].h, sphv[(q - 1, j)])
                        scalar.activation(gsb[:, M * j:M * (j + 1)], psq(sidx[s])[:, 0:M],
                                          AF.Silu).then_inc(SA.h, 1)
                    else:
                        scalar.wait_ge(VC.h, vc_h[s])
                        jj = jj_of(q, j)
                        scalar.dma_start(d_h[128 * jj:128 * (jj + 1), :],
                                         gsb[:, M * j:M * (j + 1)]).then_inc(SPH[j].h, 16)
                SA.v = eva
                for j in range(4):
                    SPH[j].v = max(sphv[s] for s in upslots if s[1] == j)

            @blk.tensor
            def _(tensor):
                tensor.wait_ge(CN.h, kc_done)
                tensor.wait_ge(SR[0].h, srv[0])
                w0 = GU_BLKS[0][1]
                for gk in range(32):
                    half = 512 * (gk % 2)
                    if gk >= 2:
                        tensor.wait_ge(VA.h, va_mul[gk - 2])
                    tensor.matmul(PS[3][:, half:half + w0], k32sel[0:32, 128 * gk:128 * (gk + 1)],
                                  srowK[0][0:32, 0:w0], start=True, stop=True, **MKW).then_inc(TB.h, 1)
                for q in range(NQ):
                    nb, mat = q // 2, q % 2
                    n0, w = GU_BLKS[nb]
                    szp = szp_gu[32:64, :] if mat else szp_gu[0:32, :]
                    xgs_h = xgsT[32:64, :] if mat else xgsT[0:32, :]
                    jt = jt_of[q]
                    nxt = None
                    if q + 1 < NQ:
                        wn = GU_BLKS[(q + 1) // 2][1]
                        per = (32 + jt - 1) // jt
                        nxt = (wn, [list(range(jx * per, min(32, (jx + 1) * per))) for jx in range(jt)])
                    tensor.wait_ge(VA.h, va_mul[32 * q + 31])
                    for j in range(jt):
                        if nxt is not None and nxt[1][j]:
                            wn = nxt[0]
                            tensor.wait_ge(SR[(q + 1) % 2].h, srv[q + 1])
                            for gk in nxt[1][j]:
                                ib = 32 * (q + 1) + gk
                                tensor.wait_ge(VA.h, va_mul[ib - 2])
                                half = 512 * (gk % 2)
                                tensor.matmul(PS[3][:, half:half + wn],
                                              k32sel[0:32, 128 * gk:128 * (gk + 1)],
                                              srowK[(q + 1) % 2][0:32, 0:wn],
                                              start=True, stop=True, **MKW).then_inc(TB.h, 1)
                        s = (q, j)
                        i = sidx[s]
                        ps = psq(i)
                        if i >= 2:
                            s2 = slots[i - 2]
                            if s2[0] % 2 == 0:
                                tensor.wait_ge(SA.h, sa_silu[s2])
                            else:
                                tensor.wait_ge(VC.h, vc_h[s2])
                        jj = jj_of(q, j)
                        for gk in range(32):
                            for mc in range(2):
                                tensor.matmul(ps[:, 512 * mc:512 * (mc + 1)],
                                              wblk[q % 2][:, gk * w + 128 * j:gk * w + 128 * (j + 1)],
                                              regA[:, M * gk + 512 * mc:M * gk + 512 * (mc + 1)],
                                              start=(gk == 0), stop=False, **MKW)
                        for mc in range(2):
                            mm = tensor.matmul(ps[:, 512 * mc:512 * (mc + 1)],
                                               szp[:, 128 * jj:128 * (jj + 1)],
                                               xgs_h[:, 512 * mc:512 * (mc + 1)],
                                               start=False, stop=(mc == 1), **MKW)
                        mm.then_inc(TA.h, 1)
                        if mat == 1:
                            ui = uidx[(q, j)] - 1
                            if ui >= 0:
                                s2 = upslots[ui]
                                tensor.wait_ge(VC.h, vc_h[s2])
                                jj2 = jj_of(*s2)
                                for mc in range(2):
                                    mm = tensor.matmul(PS[0][0:NT, 512 * mc:512 * (mc + 1)],
                                                       k86T[:, NT * jj2:NT * (jj2 + 1)],
                                                       gsb[:, M * s2[1] + 512 * mc:M * s2[1] + 512 * (mc + 1)],
                                                       start=(jj2 == 0), stop=False, **MKW)
                                mm.then_inc(TC.h, 1)
                s2 = upslots[-1]
                tensor.wait_ge(VC.h, vc_h[s2])
                jj2 = jj_of(*s2)
                for mc in range(2):
                    mm = tensor.matmul(PS[0][0:NT, 512 * mc:512 * (mc + 1)],
                                       k86T[:, NT * jj2:NT * (jj2 + 1)],
                                       gsb[:, M * s2[1] + 512 * mc:M * s2[1] + 512 * (mc + 1)],
                                       start=False, stop=True, **MKW)
                mm.then_inc(TC.h, 1)
                TA.v = ta_mm[slots[-1]]
                TB.v = tb_sbc[-1]
                TC.v = tc_hgs[upslots[-1]]

    stkR2.close()
    stkRA.close()

    if UPTO <= 4:
        return nc

    with ExitStack() as pe:
        k86sel = pe.enter_context(nc.sbuf_tensor("k86sel", [NT, NT * 128], BF16))
        wdb = [pe.enter_context(nc.sbuf_tensor("wdb%d" % i, [128, NT * 384], BF16)) for i in range(2)]
        bbE = [pe.enter_context(nc.sbuf_tensor("bbE%d" % i, [128, 16 * 384], U8)) for i in range(2)]
        srE = [pe.enter_context(nc.sbuf_tensor("srE%d" % i, [NT, 384], BF16)) for i in range(2)]
        hch = [pe.enter_context(nc.sbuf_tensor("hch%d" % i, [128, 4 * M], BF16)) for i in range(2)]
        fst = [pe.enter_context(nc.sbuf_tensor("fst%d" % i, [128, 512], BF16)) for i in range(2)]
        fdbg = pe.enter_context(nc.sbuf_tensor("fdbg", [128, 512], BF16)) if DEBUG else None

        dtq = [DN_BLKS[db][1] // 128 for db in range(NDB)]
        gdt_base = [DN_BLKS[db][0] // 128 for db in range(NDB)]
        hchunks = [(n0, min(4, NT - n0)) for n0 in range(0, NT, 4)]
        bchunks = [(g0, min(16, NT - g0)) for g0 in range(0, NT, 16)]
        NHC = len(hchunks)
        NBC = len(bchunks)

        tb_sbcE = {}
        ev = TB.v
        for db in range(NDB):
            for gk in range(NT):
                ev += 1
                tb_sbcE[(db, gk)] = ev
        va_mulE = {}
        ev = VA.v
        for db in range(NDB):
            for gk in range(NT):
                ev += 1
                va_mulE[(db, gk)] = ev
        ta_db = [TA.v + db + 1 for db in range(NDB)]
        tc_chk = {}
        ev = TC.v
        for db in range(NDB):
            for c in range(NHC):
                ev += 1
                tc_chk[(db, c)] = ev
        sa_ev, spfv = {}, {}
        eva = SA.v
        evlist = []
        _cnt = [0, 0]
        for db in range(NDB):
            for dt in range(dtq[db]):
                for mh in range(2):
                    eva += 1
                    sa_ev[(db, dt, mh)] = eva
                    k = len(evlist)
                    p = k % 2
                    _cnt[p] += 1
                    spfv[(db, dt, mh)] = SPF[p].v + 16 * _cnt[p]
                    evlist.append((db, dt, mh))
        bev = {}
        _cnt = [0, 0]
        for db in range(NDB):
            for bc in range(NBC):
                p = bc % 2
                _cnt[p] += 1
                bev[(db, bc)] = BE[p].v + 32 * _cnt[p]
        hhv = {}
        _cnt = [0, 0]
        for db in range(NDB):
            for c in range(NHC):
                gi = db * NHC + c
                p = gi % 2
                _cnt[p] += 1
                hhv[(db, c)] = HH[p].v + 16 * _cnt[p]
        srvE = [SR[db % 2].v + 16 * (db // 2 + 1) for db in range(NDB)]
        k86c_done = CN.v + 16
        sph_tot = [SPH[j].v for j in range(4)]

        def psd(dt, mh):
            return [PS[1], PS[2], PS[0]][dt][:, 512 * mh:512 * mh + 512]

        h_re = d_h.rearrange("(n p) m -> p n m", p=128)

        with nc.Block() as blk:
            @blk.sync
            def _(sync):
                sync.dma_start(k86sel[:, :], d_k86sel).then_inc(CN.h, 16)
                CN.v = k86c_done
                for db in range(NDB):
                    d0, w = DN_BLKS[db]
                    src_all = d_db[dn_boff[db]:dn_boff[db] + 64 * NT * w].rearrange("(p f) -> p f", p=64)
                    for bc, (g0, cnt) in enumerate(bchunks):
                        gi = db * NBC + bc
                        if gi >= 2:
                            pdb, pbc = divmod(gi - 2, NBC)
                            lastg = bchunks[pbc][0] + bchunks[pbc][1] - 1
                            sync.wait_ge(VA.h, va_mulE[(pdb, lastg)])
                        src = src_all[:, g0 * w:(g0 + cnt) * w]
                        sync.dma_start(bbE[bc % 2][0:64, 0:cnt * w], src).then_inc(BE[bc % 2].h, 16)
                        sync.dma_start(bbE[bc % 2][64:128, 0:cnt * w], src).then_inc(BE[bc % 2].h, 16)
                BE[0].v = max(v for k2, v in bev.items() if k2[1] % 2 == 0)
                BE[1].v = max(v for k2, v in bev.items() if k2[1] % 2 == 1)

            @blk.gpsimd
            def _(g):
                for j in range(4):
                    g.wait_ge(SPH[j].h, sph_tot[j])
                def sre(db):
                    d0, w = DN_BLKS[db]
                    if db >= 2:
                        g.wait_ge(TB.h, tb_sbcE[(db - 1, 0)] - 1)
                    g.dma_start(srE[db % 2][0:NT, 0:w], d_dsc[0:NT, d0:d0 + w]).then_inc(SR[db % 2].h, 16)
                sre(0)
                sre(1)
                for db in range(NDB):
                    for c, (n0, cnt) in enumerate(hchunks):
                        gi = db * NHC + c
                        if gi >= 2:
                            pdb, pc2 = divmod(gi - 2, NHC)
                            g.wait_ge(TC.h, tc_chk[(pdb, pc2)])
                        g.dma_start(hch[gi % 2][:, 0:cnt * M].rearrange("p (n m) -> p n m", m=M),
                                    h_re[:, n0:n0 + cnt, :]).then_inc(HH[gi % 2].h, 16)
                    if db + 2 < NDB:
                        sre(db + 2)
                SR[0].v, SR[1].v = srvE[-2], srvE[-1]
                HH[0].v = max(v for k2, v in hhv.items() if (k2[0] * NHC + k2[1]) % 2 == 0)
                HH[1].v = max(v for k2, v in hhv.items() if (k2[0] * NHC + k2[1]) % 2 == 1)

            @blk.vector
            def _(vector):
                for db in range(NDB):
                    d0, w = DN_BLKS[db]
                    if db >= 2:
                        vector.wait_ge(TA.h, ta_db[db - 2])
                    for bc, (g0, cnt) in enumerate(bchunks):
                        vector.wait_ge(BE[bc % 2].h, bev[(db, bc)])
                        vector.tensor_scalar(bbE[bc % 2][0:64, 0:cnt * w], bbE[bc % 2][0:64, 0:cnt * w],
                                             15, None, ALU.bitwise_and)
                        vector.tensor_scalar(bbE[bc % 2][64:128, 0:cnt * w], bbE[bc % 2][64:128, 0:cnt * w],
                                             4, None, ALU.logical_shift_right)
                        for gi2 in range(cnt):
                            gk = g0 + gi2
                            vector.wait_ge(TB.h, tb_sbcE[(db, gk)])
                            half = 512 * (gk % 2)
                            vector.tensor_tensor(wdb[db % 2][:, gk * w:(gk + 1) * w],
                                                 bbE[bc % 2][:, gi2 * w:(gi2 + 1) * w],
                                                 PS[3][:, half:half + w], ALU.mult).then_inc(VA.h, 1)
                VA.v = va_mulE[(NDB - 1, NT - 1)]

            @blk.scalar
            def _(scalar):
                def spill(k):
                    db2, dt2, mh2 = evlist[k]
                    gdt = gdt_base[db2] + dt2
                    scalar.dma_start(d_ffn[128 * gdt:128 * (gdt + 1), 512 * mh2:512 * mh2 + 512],
                                     fst[k % 2][:, :]).then_inc(SPF[k % 2].h, 16)
                import os as _os2
                EDLY = int(_os2.environ.get("EDLY", "3"))
                for k, (db, dt, mh) in enumerate(evlist):
                    scalar.wait_ge(TA.h, ta_db[db])
                    if dt == 0 and mh == 0:
                        for _ in range(EDLY):
                            scalar.copy(fst[k % 2][:, :], psd(dt, mh))
                    if k >= 2:
                        scalar.wait_ge(SPF[k % 2].h, spfv[evlist[k - 2]])
                    scalar.copy(fst[k % 2][:, :], psd(dt, mh)).then_inc(SA.h, 1)
                    if DEBUG and k == 2:
                        scalar.copy(fdbg[:, :], psd(0, 0))
                        scalar.dma_start(dbg["fst0"], fdbg[:, :]).then_inc(DBGS.h, 16)
                        DBGS.v += 16
                    if k >= 1:
                        spill(k - 1)
                spill(len(evlist) - 1)
                SA.v = sa_ev[evlist[-1]]
                if DEBUG:
                    pass
                SPF[0].v = max(v for k2, v in spfv.items() if evlist.index(k2) % 2 == 0)
                SPF[1].v = max(v for k2, v in spfv.items() if evlist.index(k2) % 2 == 1)

            @blk.tensor
            def _(tensor):
                tensor.wait_ge(CN.h, k86c_done)
                tensor.wait_ge(SR[0].h, srvE[0])
                w0 = DN_BLKS[0][1]
                for gk in range(NT):
                    half = 512 * (gk % 2)
                    if gk >= 2:
                        tensor.wait_ge(VA.h, va_mulE[(0, gk - 2)])
                    tensor.matmul(PS[3][:, half:half + w0], k86sel[0:NT, 128 * gk:128 * (gk + 1)],
                                  srE[0][0:NT, 0:w0], start=True, stop=True, **MKW).then_inc(TB.h, 1)
                import os as _os
                EBC = int(_os.environ.get("EBC", "1"))
                for db in range(NDB):
                    d0, w = DN_BLKS[db]
                    tensor.wait_ge(VA.h, va_mulE[(db, NT - 1)])
                    if db >= 1:
                        tensor.wait_ge(SA.h, sa_ev[(db - 1, dtq[db - 1] - 1, 1)])
                    if db + 1 < NDB:
                        wn = DN_BLKS[db + 1][1]
                        if EBC:
                            per = (NT + NHC - 1) // NHC
                        else:
                            per = NT
                        nb_batches = [list(range(c * per, min(NT, (c + 1) * per))) for c in range(NHC)]
                        if not EBC:
                            tensor.wait_ge(SR[(db + 1) % 2].h, srvE[db + 1])
                            for gk in range(NT):
                                half = 512 * (gk % 2)
                                ibw = va_mulE[(db + 1, gk)] - 2
                                if ibw >= 1:
                                    tensor.wait_ge(VA.h, ibw)
                                tensor.matmul(PS[3][:, half:half + wn],
                                              k86sel[0:NT, 128 * gk:128 * (gk + 1)],
                                              srE[(db + 1) % 2][0:NT, 0:wn],
                                              start=True, stop=True, **MKW).then_inc(TB.h, 1)
                            nb_batches = [[] for _ in range(NHC)]
                    for c, (n0, cnt) in enumerate(hchunks):
                        if db + 1 < NDB and nb_batches[c]:
                            tensor.wait_ge(SR[(db + 1) % 2].h, srvE[db + 1])
                            for gk in nb_batches[c]:
                                half = 512 * (gk % 2)
                                ibw = va_mulE[(db + 1, gk)] - 2
                                if ibw >= 1:
                                    tensor.wait_ge(VA.h, ibw)
                                tensor.matmul(PS[3][:, half:half + wn],
                                              k86sel[0:NT, 128 * gk:128 * (gk + 1)],
                                              srE[(db + 1) % 2][0:NT, 0:wn],
                                              start=True, stop=True, **MKW).then_inc(TB.h, 1)
                        gi = db * NHC + c
                        tensor.wait_ge(HH[gi % 2].h, hhv[(db, c)])
                        for ni in range(cnt):
                            gk = n0 + ni
                            for dt in range(dtq[db]):
                                for mh in range(2):
                                    mm = tensor.matmul(psd(dt, mh),
                                                       wdb[db % 2][:, gk * w + 128 * dt:gk * w + 128 * (dt + 1)],
                                                       hch[gi % 2][:, M * ni + 512 * mh:M * ni + 512 * (mh + 1)],
                                                       start=(gk == 0), stop=False, **MKW)
                        mm.then_inc(TC.h, 1)
                    for dt in range(dtq[db]):
                        for mh in range(2):
                            mm = tensor.matmul(psd(dt, mh),
                                               szpd[0:NT, d0 + 128 * dt:d0 + 128 * (dt + 1)],
                                               hgs_sb[0:NT, 512 * mh:512 * (mh + 1)],
                                               start=False, stop=True, **MKW)
                    mm.then_inc(TA.h, 1)
                TA.v = ta_db[-1]
                TB.v = tb_sbcE[(NDB - 1, NT - 1)]
                TC.v = tc_chk[(NDB - 1, NHC - 1)]

    if UPTO <= 5:
        return nc

    with ExitStack() as pf:
        fb1 = [pf.enter_context(nc.sbuf_tensor("fb1_%d" % i, [128, M], BF16)) for i in range(2)]
        fsq = [pf.enter_context(nc.sbuf_tensor("fsq%d" % i, [128, M], BF16)) for i in range(2)]
        fb2 = [pf.enter_context(nc.sbuf_tensor("fb2_%d" % i, [128, M], BF16)) for i in range(2)]
        o1c = [pf.enter_context(nc.sbuf_tensor("o1c%d" % i, [128, M], F32)) for i in range(2)]
        ost = [pf.enter_context(nc.sbuf_tensor("ost%d" % i, [128, M], F32)) for i in range(2)]
        bcAf = pf.enter_context(nc.sbuf_tensor("bcA_f", [128, M], F32))
        bcBf = pf.enter_context(nc.sbuf_tensor("bcB_f", [128, M], F32))
        alloc_rows(pf, "f")

        sa_sqf = [SA.v + t + 1 for t in range(DT)]
        ta_stf = [TA.v + t + 1 for t in range(DT)]
        vc_rows4 = VC.v + 1
        sb_rstd4 = SB.v + 1
        tb_bc4 = TB.v + 1
        va_out = [VA.v + t + 1 for t in range(DT)]
        fbv = [FB[t % 2].v + 16 * (t // 2 + 1) for t in range(DT)]
        fcv = [FC[t % 2].v + 32 * (t // 2 + 1) for t in range(DT)]
        gpov = [GPO[t % 2].v + 16 * (t // 2 + 1) for t in range(DT)]
        spf_tot = [SPF[0].v, SPF[1].v]
        so_tot2 = [SO[0].v, SO[1].v]

        with nc.Block() as blk:
            @blk.sync
            def _(sync):
                sync.wait_ge(SPF[0].h, spf_tot[0])
                sync.wait_ge(SPF[1].h, spf_tot[1])
                for t in range(DT):
                    if t >= 2:
                        sync.wait_ge(TA.h, ta_stf[t - 2])
                    sync.dma_start(fb1[t % 2][:, :], d_ffn[128 * t:128 * (t + 1), :]).then_inc(FB[t % 2].h, 16)
                FB[0].v, FB[1].v = fbv[-2], fbv[-1]
                sync.wait_ge(SO[0].h, so_tot2[0])
                sync.wait_ge(SO[1].h, so_tot2[1])
                for t in range(DT):
                    if t >= 2:
                        sync.wait_ge(VA.h, va_out[t - 2])
                    sync.dma_start(fb2[t % 2][:, :], d_ffn[128 * t:128 * (t + 1), :]).then_inc(FC[t % 2].h, 16)
                    sync.dma_start(o1c[t % 2][:, :], d_o1[128 * t:128 * (t + 1), :]).then_inc(FC[t % 2].h, 16)
                FC[0].v, FC[1].v = fcv[-2], fcv[-1]

            @blk.scalar
            def _(scalar):
                for t in range(DT):
                    scalar.wait_ge(FB[t % 2].h, fbv[t])
                    if t >= 2:
                        scalar.wait_ge(TA.h, ta_stf[t - 2])
                    scalar.activation(fsq[t % 2][:, :], fb1[t % 2][:, :], AF.Square).then_inc(SA.h, 1)
                SA.v = sa_sqf[-1]
                scalar.wait_ge(VC.h, vc_rows4)
                rows_s(scalar)

            @blk.tensor
            def _(tensor):
                for t in range(DT):
                    tensor.wait_ge(SA.h, sa_sqf[t])
                    stat_mms(tensor, fb1[t % 2], fsq[t % 2], t == 0, t == DT - 1).then_inc(TA.h, 1)
                TA.v = ta_stf[-1]
                tensor.wait_ge(SB.h, sb_rstd4)
                tensor.wait_ge(VC.h, vc_rows4)
                rows_t(tensor)

            @blk.vector
            def _(vector):
                vector.wait_ge(TA.h, ta_stf[-1])
                rows_v(vector, D)
                vector.wait_ge(TB.h, tb_bc4)
                bc_evac(vector, bcAf, bcBf)
                for t in range(DT):
                    vector.wait_ge(FC[t % 2].h, fcv[t])
                    if t >= 2:
                        vector.wait_ge(GPO[t % 2].h, gpov[t - 2])
                    vector.tensor_tensor(ost[t % 2][:, :], fb2[t % 2][:, :], bcBf[:, :], ALU.add)
                    vector.tensor_tensor(ost[t % 2][:, :], ost[t % 2][:, :], bcAf[:, :], ALU.mult)
                    vector.tensor_tensor(ost[t % 2][:, :], ost[t % 2][:, :], o1c[t % 2][:, :],
                                         ALU.add).then_inc(VA.h, 1)
                VA.v = va_out[-1]

            @blk.gpsimd
            def _(g):
                for t in range(DT):
                    g.wait_ge(VA.h, va_out[t])
                    g.dma_start(d_out[128 * t:128 * (t + 1), :], ost[t % 2][:, :]).then_inc(GPO[t % 2].h, 16)
                GPO[0].v, GPO[1].v = gpov[-2], gpov[-1]
                g.wait_ge(GPO[0].h, GPO[0].v)
                g.wait_ge(GPO[1].h, GPO[1].v)

    stk.close()
    return nc


_NC_CACHE = None


def kernel(**inputs):
    global _NC_CACHE
    per_core = host_prep(inputs)
    if _NC_CACHE is None:
        _NC_CACHE = build_nc()
    res = run_bass_kernel_spmd(_NC_CACHE, per_core, core_ids=list(range(NCORES)))
    out = np.empty((8192, D), dtype=np.float32)
    for c in range(NCORES):
        out[M * c:M * (c + 1), :][:, P4] = res.results[c]["out"].T
    return out.reshape(2, 4096, 4096)


